# version 02
# baseline (speedup 1.0000x reference)
import contextlib
import numpy as np
import ml_dtypes
import concourse.bass as bass
import concourse.mybir as mybir
from concourse.bass_utils import run_bass_kernel_spmd

F32 = mybir.dt.float32
BF16 = mybir.dt.bfloat16
AF = mybir.ActivationFunctionType
ALU = mybir.AluOpType
AX = mybir.AxisListType

NCORES = 8
D = 2048
I = 4096
T = 4096
KC = 16
IC = 32
TL = 256
NT = T // TL
MAGIC = 12582912.0
C_D = float(1.0 / np.sqrt(np.float32(2048.0)))
C_I = 1.0 / 64.0
ENGS = ("sync", "gpsimd", "tensor", "vector", "scalar")


class Prog:
    def __init__(self, nc):
        self.nc = nc
        self.ops = {e: [] for e in ENGS}
        self.sems = {}
        self.cnt = {}
        self.es = contextlib.ExitStack()

    def sem(self, name):
        if name not in self.sems:
            self.sems[name] = self.es.enter_context(self.nc.semaphore(name))
            self.cnt[name] = 0
        return self.sems[name]

    def wait(self, eng, name, val=None):
        s = self.sem(name)
        v = self.cnt[name] if val is None else val
        if v > 0:
            self.ops[eng].append(lambda E, s=s, v=v: E.wait_ge(s, v))

    def dma(self, eng, sem, out, in_, transpose=False):
        s = self.sem(sem)
        self.cnt[sem] += 16
        if transpose:
            self.ops[eng].append(lambda E, o=out, i=in_, s=s: E.dma_start(
                out=o, in_=i, transpose=True).then_inc(s, 16))
        else:
            self.ops[eng].append(lambda E, o=out, i=in_, s=s: E.dma_start(
                out=o, in_=i).then_inc(s, 16))
        return self.cnt[sem]

    def op(self, eng, fn, sem=None, inc=1):
        if sem is None:
            self.ops[eng].append(fn)
            return None
        s = self.sem(sem)
        self.cnt[sem] += inc
        self.ops[eng].append(lambda E, f=fn, s=s, n=inc: f(E).then_inc(s, n))
        return self.cnt[sem]


def bfly(p, src3, dst3, nch, h):
    n = nch // (2 * h)
    sv = src3.rearrange("p (n t h) w -> p n t h w", t=2, h=h)
    dv = dst3.rearrange("p (n t h) w -> p n t h w", t=2, h=h)
    p.op("vector", lambda E: E.tensor_tensor(
        dv[:, :, 0], sv[:, :, 0], sv[:, :, 1], ALU.add), sem="vec")
    p.op("vector", lambda E: E.tensor_tensor(
        dv[:, :, 1], sv[:, :, 0], sv[:, :, 1], ALU.subtract), sem="vec")


def bc3(ap2, n):
    return ap2.rearrange("p (o w) -> p o w", o=1).broadcast_to(
        (128, n, ap2.shape[-1]))


def bcc(ap2, n):
    return ap2.rearrange("p (c w) -> p c w", w=1).broadcast_to(
        (128, ap2.shape[-1], n))


def build_nc(debug=False):
    nc = bass.Bass("TRN2", target_bir_lowering=False, debug=False)
    p = Prog(nc)
    es = p.es

    x_in = nc.dram_tensor("x", [D, T], F32, kind="ExternalInput").ap()
    W_in = nc.dram_tensor("W_in", [D, D], F32, kind="ExternalInput").ap()
    W_dec = nc.dram_tensor("W_dec", [D, D], F32, kind="ExternalInput").ap()
    W_gate = nc.dram_tensor("W_gate", [D, D], F32, kind="ExternalInput").ap()
    W_out = nc.dram_tensor("W_out", [D, D], F32, kind="ExternalInput").ap()
    w_upT = nc.dram_tensor("w_upT", [D, I], F32, kind="ExternalInput").ap()
    w_dnT = nc.dram_tensor("w_dnT", [I, D], F32, kind="ExternalInput").ap()
    b_in = nc.dram_tensor("b_in", [128, KC], F32, kind="ExternalInput").ap()
    b_dec = nc.dram_tensor("b_dec", [128, KC], F32, kind="ExternalInput").ap()
    b_dbias = nc.dram_tensor("b_dbias", [128, KC], F32, kind="ExternalInput").ap()
    b_gate = nc.dram_tensor("b_gate", [128, KC], F32, kind="ExternalInput").ap()
    b_out = nc.dram_tensor("b_out", [128, KC], F32, kind="ExternalInput").ap()
    H128_in = nc.dram_tensor("H128", [128, 128], BF16, kind="ExternalInput").ap()
    identb_in = nc.dram_tensor("identb", [128, 128], BF16, kind="ExternalInput").ap()
    identf_in = nc.dram_tensor("identf", [128, 128], F32, kind="ExternalInput").ap()
    sel_in = nc.dram_tensor("sel", [128, NCORES], F32, kind="ExternalInput").ap()

    y_out = nc.dram_tensor("y", [T, D], F32, kind="ExternalOutput").ap()

    def scratch(name, shape, dtype):
        kind = "ExternalOutput" if debug else "Internal"
        return nc.dram_tensor(name, shape, dtype, kind=kind).ap()

    xT_sp = scratch("xT_sp", [D, T], BF16)
    s0_sp = scratch("s0_sp", [D, T], BF16)
    cp_sp = scratch("cp_sp", [D, T], BF16)
    y_sp = scratch("y_sp", [D, T], BF16)
    xhat_sp = scratch("xhat_sp", [D, T], F32)
    xhatb_sp = scratch("xhatb_sp", [D, T], BF16)
    xq2_sp = scratch("xq2_sp", [I, T], BF16)
    fold_sp = scratch("fold_sp", [NT, 128, 2], F32)
    if debug:
        u_dbg = nc.dram_tensor("u_dbg", [D, T], F32, kind="ExternalOutput").ap()
        i_dbg = nc.dram_tensor("i_dbg", [D, T], BF16, kind="ExternalOutput").ap()
        dm_dbg = nc.dram_tensor("dm_dbg", [D, T], F32, kind="ExternalOutput").ap()
        u_dbg_v = u_dbg.rearrange("(c p) t -> p c t", p=128)
        init_dbg = nc.dram_tensor("init_dbg", [128, KC], F32, kind="ExternalOutput").ap()
        g_dbg = nc.dram_tensor("g_dbg", [D, TL], BF16, kind="ExternalOutput").ap()
        ag_dbg = nc.dram_tensor("ag_dbg", [128, NCORES, 2 * KC], F32, kind="ExternalOutput").ap()
        xtb_dbg = nc.dram_tensor("xtb_dbg", [128, KC, TL], BF16, kind="ExternalOutput").ap()
        wqd_dbg = nc.dram_tensor("wqd_dbg", [128, IC, D], BF16, kind="ExternalOutput").ap()
        alpha_dbg = nc.dram_tensor("alpha_dbg", [128, 4], F32, kind="ExternalOutput").ap()
        i_dbg_v = i_dbg.rearrange("(c p) t -> p c t", p=128)
        dm_dbg_v = dm_dbg.rearrange("(c p) t -> p c t", p=128)
    cc_in = nc.dram_tensor("cc_in", [128, 2 * KC], F32, kind="Internal").ap()
    cc_out = nc.dram_tensor("cc_out", [NCORES, 128, 2 * KC], F32,
                            kind="Internal").ap()

    def fvw(ap2d):
        return ap2d.rearrange("(c p) t -> p c t", p=128)

    V = dict(xT=fvw(xT_sp), s0=fvw(s0_sp), cp=fvw(cp_sp), y=fvw(y_sp),
             xhat=fvw(xhat_sp), xhatb=fvw(xhatb_sp), xq2=fvw(xq2_sp))

    def sb(name, shape, dt):
        return es.enter_context(nc.sbuf_tensor(name, shape, dt))

    C = dict(
        bin=sb("c_bin", [128, KC], F32)[:],
        bdec=sb("c_bdec", [128, KC], F32)[:],
        bdecn=sb("c_bdecn", [128, KC], F32)[:],
        bgate=sb("c_bgate", [128, KC], F32)[:],
        bout=sb("c_bout", [128, KC], F32)[:],
        H=sb("c_H", [128, 128], BF16)[:],
        idb=sb("c_idb", [128, 128], BF16)[:],
        idf=sb("c_idf", [128, 128], F32)[:],
        sel=sb("c_sel", [128, NCORES], F32)[:],
        ones1=sb("c_ones1", [1, 128], F32)[:],
        onesc=sb("c_onesc", [128, 1], BF16)[:],
        onescf=sb("c_onescf", [128, 1], F32)[:],
        init=sb("c_init", [128, KC], F32)[:],
        alpha=sb("c_alpha", [128, 4], F32)[:],
        scar=sb("c_scar", [128, KC], F32)[:],
        pcar=sb("c_pcar", [128, KC], F32)[:],
    )
    arb_t = sb("arena_b", [128, 86016], BF16)
    arf_t = sb("arena_f", [128, 9216], F32)

    def ARB(off, chunks, width):
        v = arb_t[:, off:off + chunks * width]
        return v if chunks == 1 else v.rearrange("p (c w) -> p c w", c=chunks)

    def ARF(off, chunks, width):
        v = arf_t[:, off:off + chunks * width]
        return v if chunks == 1 else v.rearrange("p (c w) -> p c w", c=chunks)

    def ARFR(off, width, nparts=1):
        return arf_t[0:nparts, off:off + width]

    PBIG = es.enter_context(nc.psum_tensor("pbig", [128, 2048], F32))[:]
    PS2 = es.enter_context(nc.psum_tensor("ps2", [128, 1024], F32))[:]
    PS_ROW = es.enter_context(nc.psum_tensor("ps_row", [1, 1024], F32))[:]

    p.dma("sync", "ld0", C["bin"], b_in)
    p.dma("sync", "ld0", C["bdec"], b_dec)
    p.dma("sync", "ld0", C["bgate"], b_gate)
    p.dma("sync", "ld0", C["bout"], b_out)
    p.dma("sync", "ld0", C["H"], H128_in)
    p.dma("sync", "ld0", C["idb"], identb_in)
    p.dma("sync", "ld0", C["idf"], identf_in)
    p.dma("sync", "ld0", C["sel"], sel_in)
    p.dma("sync", "ld0", C["bdecn"], b_dbias)
    p.wait("vector", "ld0")
    p.op("vector", lambda E: E.tensor_tensor(C["bdec"], C["bdec"], C["bdecn"], ALU.add), sem="vec")
    p.wait("vector", "vec", p.cnt["vec"])
    p.op("vector", lambda E: E.tensor_scalar(C["bdecn"], C["bdec"], -1.0, None, ALU.mult), sem="vec")
    p.op("vector", lambda E: E.memset(C["ones1"], 1.0), sem="vec")
    p.op("vector", lambda E: E.memset(C["onesc"], 1.0), sem="vec")
    p.op("vector", lambda E: E.memset(C["onescf"], 1.0), sem="vec")
    p.op("vector", lambda E: E.memset(C["scar"], 0.0), sem="vec")
    p.op("vector", lambda E: E.memset(C["pcar"], 1.0), sem="vec")

    p.dma("gpsimd", "gcast", xT_sp, x_in)
    p.wait("sync", "gcast")

    w_in_b = ARB(0, KC, D)
    w_dec_b = ARB(KC * D, KC, D)
    xt_a = [ARB(2 * KC * D + i * KC * 512, KC, 512) for i in range(2)]
    inp_a = [ARB(2 * KC * D + 2 * KC * 512 + i * 1024, 1, 1024) for i in range(2)]
    d_a = [ARF(i * 1024, 1, 1024) for i in range(2)]
    dm_a = [ARF(2048 + i * 1024, 1, 1024) for i in range(2)]
    u_a = [ARF(4096 + i * 1024, 1, 1024) for i in range(2)]
    s0b = ARF(6144, 1, 1024)
    cpb = ARF(7168, 1, 1024)
    zeros_a = ARF(8192, 1, 1024)

    p.dma("gpsimd", "gw", w_in_b, W_in.rearrange("(k p) e -> p k e", p=128))
    p.dma("gpsimd", "gw", w_dec_b, W_dec.rearrange("(k p) e -> p k e", p=128))
    p.op("vector", lambda E: E.memset(zeros_a, 0.0), sem="vec")
    p.wait("tensor", "gw")

    free_bank = {}
    free_xt = [0, 0]
    free_du = {}
    L = 1024
    for s in range(4):
        for e in range(KC):
            par = e % 2
            d_t, dm_t, u_t, i_t = d_a[par], dm_a[par], u_a[par], inp_a[par]
            for tau in range(2):
                tile = s * 2 + tau
                ldsem = "ld%d" % (tile % 2)
                if e == 0:
                    if free_xt[tile % 2]:
                        p.wait("sync", "mm", free_xt[tile % 2])
                    p.dma("sync", ldsem, xt_a[tile % 2],
                          V["xT"][:, :, tile * 512:(tile + 1) * 512])
                    p.wait("tensor", ldsem)
                bk_in = PBIG[:, (2 * tau) * 512:(2 * tau) * 512 + 512]
                bk_d = PBIG[:, (2 * tau + 1) * 512:(2 * tau + 1) * 512 + 512]
                for slot in (2 * tau, 2 * tau + 1):
                    if slot in free_bank:
                        p.wait("tensor", "scl", free_bank[slot])
                xtile = xt_a[tile % 2]
                for k in range(KC):
                    p.op("tensor", lambda E, k=k, e=e, x=xtile, bk=bk_in: E.matmul(
                        bk, w_in_b[:, k, e * 128:(e + 1) * 128], x[:, k, :],
                        start=(k == 0), stop=(k == KC - 1)))
                for k in range(KC):
                    last = k == KC - 1
                    p.op("tensor", lambda E, k=k, e=e, x=xtile, bk=bk_d: E.matmul(
                        bk, w_dec_b[:, k, e * 128:(e + 1) * 128], x[:, k, :],
                        start=(k == 0), stop=last), sem="mm" if last else None)
                if e == KC - 1:
                    free_xt[tile % 2] = p.cnt["mm"]
                p.wait("scalar", "mm", p.cnt["mm"])
                if tau == 0 and par in free_du:
                    p.wait("scalar", "vec", free_du[par])
                    if debug:
                        p.wait("scalar", "gsp")
                sl = slice(tau * 512, tau * 512 + 512)
                p.op("scalar", lambda E, b=bk_in, o=i_t, sl=sl, e=e: E.activation(
                    o[:, sl], b, AF.Silu, bias=C["bin"][:, e:e + 1], scale=np.float32(1.0)), sem="scl")
                p.op("scalar", lambda E, b=bk_d, o=d_t, sl=sl, e=e: E.activation(
                    o[:, sl], b, AF.Sigmoid, bias=C["bdec"][:, e:e + 1], scale=np.float32(1.0)), sem="scl")
                p.op("scalar", lambda E, b=bk_d, o=dm_t, sl=sl, e=e: E.activation(
                    o[:, sl], b, AF.Sigmoid, bias=C["bdecn"][:, e:e + 1], scale=np.float32(-1.0)), sem="scl")
                free_bank[2 * tau] = p.cnt["scl"]
                free_bank[2 * tau + 1] = p.cnt["scl"]
            p.wait("vector", "scl", p.cnt["scl"])
            p.op("vector", lambda E, u=u_t, dm=dm_t, i=i_t: E.tensor_tensor(
                u, dm, i, ALU.mult), sem="vec")
            if (s, e) != (0, 0):
                p.wait("vector", "gsp")
            p.op("vector", lambda E, d=d_t, u=u_t, e=e: E.tensor_tensor_scan(
                s0b, d, u, C["scar"][:, e:e + 1], ALU.mult, ALU.add), sem="vec")
            p.op("vector", lambda E, d=d_t, e=e: E.tensor_tensor_scan(
                cpb, d, zeros_a, C["pcar"][:, e:e + 1], ALU.mult, ALU.add), sem="vec")
            p.wait("vector", "vec", p.cnt["vec"])
            p.op("vector", lambda E, e=e: E.tensor_copy(
                C["scar"][:, e:e + 1], s0b[:, L - 1:L]), sem="vec")
            p.op("vector", lambda E, e=e: E.tensor_copy(
                C["pcar"][:, e:e + 1], cpb[:, L - 1:L]), sem="vec")
            free_du[par] = p.cnt["vec"]
            p.wait("gpsimd", "vec", p.cnt["vec"])
            p.dma("gpsimd", "gsp", V["s0"][:, e, s * L:(s + 1) * L], s0b)
            p.dma("gpsimd", "gsp", V["cp"][:, e, s * L:(s + 1) * L], cpb)
            if debug:
                p.dma("gpsimd", "gsp", u_dbg_v[:, e, s * L:(s + 1) * L], u_t)
                p.dma("gpsimd", "gsp", i_dbg_v[:, e, s * L:(s + 1) * L], i_t)
                p.dma("gpsimd", "gsp", dm_dbg_v[:, e, s * L:(s + 1) * L], dm_t)
    p.wait("sync", "vec", p.cnt["vec"])
    p.dma("sync", "sp", cc_in[:, 0:KC], C["pcar"])
    p.dma("sync", "sp", cc_in[:, KC:2 * KC], C["scar"])

    ag = ARF(0, NCORES, 2 * KC)
    ccs = [ARF(256, 1, KC), ARF(256 + KC, 1, KC)]
    acs = [ARF(256 + 2 * KC, 1, KC), ARF(256 + 3 * KC, 1, KC)]
    tmpx = ARF(256 + 4 * KC, 1, KC)
    p.wait("gpsimd", "sp")
    p.op("gpsimd", lambda E: E.collective_compute(
        "AllGather", ALU.bypass, replica_groups=[list(range(NCORES))],
        ins=[cc_in], outs=[cc_out]), sem="cc")
    p.wait("sync", "cc", p.cnt["cc"])
    p.dma("sync", "ld0", ag, cc_out.rearrange("g p t -> p g t"))
    p.wait("vector", "ld0")
    p.op("vector", lambda E: E.memset(ccs[0], 0.0), sem="vec")
    p.op("vector", lambda E: E.memset(acs[0], 0.0), sem="vec")
    for j in range(NCORES):
        cj, cj1 = ccs[j % 2], ccs[(j + 1) % 2]
        aj, aj1 = acs[j % 2], acs[(j + 1) % 2]
        if j == 4:
            p.op("vector", lambda E, c=cj: E.memset(c, 0.0), sem="vec")
        p.wait("vector", "vec", p.cnt["vec"])
        p.op("vector", lambda E, c=cj, a=aj, o=aj1, j=j: E.scalar_tensor_tensor(
            o, c, C["sel"][:, j:j + 1], a, ALU.mult, ALU.add), sem="vec")
        p.wait("vector", "vec", p.cnt["vec"])
        p.op("vector", lambda E, c=cj, j=j: E.tensor_tensor(
            tmpx, ag[:, j, 0:KC], c, ALU.mult), sem="vec")
        p.wait("vector", "vec", p.cnt["vec"])
        p.op("vector", lambda E, o=cj1, j=j: E.tensor_tensor(
            o, tmpx, ag[:, j, KC:2 * KC], ALU.add), sem="vec")
        p.wait("vector", "vec", p.cnt["vec"])
    p.op("vector", lambda E, a=acs[NCORES % 2]: E.tensor_copy(C["init"], a), sem="vec")
    if debug:
        p.wait("sync", "vec", p.cnt["vec"])
        p.dma("sync", "sp", init_dbg, C["init"])
        p.dma("sync", "sp", ag_dbg, ag)

    wg = ARB(0, KC, D)
    o = KC * D
    xt_b = [ARB(o, KC, TL), ARB(o + KC * TL, KC, TL)]
    o += 2 * KC * TL
    s0_b = [ARB(o, KC, TL), ARB(o + KC * TL, KC, TL)]
    o += 2 * KC * TL
    cp_b = [ARB(o, KC, TL), ARB(o + KC * TL, KC, TL)]
    o += 2 * KC * TL
    gt_b = ARB(o, KC, TL)
    yt_b = ARB(o + KC * TL, KC, TL)
    tmp16 = ARF(0, KC, TL)

    p.wait("gpsimd", "mm", p.cnt["mm"])
    p.dma("gpsimd", "gw", wg, W_gate.rearrange("(k p) e -> p k e", p=128))
    p.wait("tensor", "gw")
    p.wait("sync", "gsp")
    free_bank = {}
    free_ld = [0, 0]
    free_g = 0
    ld_mark = [0] * (NT + 1)

    def b1_load(t):
        par = t % 2
        if free_ld[par]:
            p.wait("sync", "vec", free_ld[par])
        p.dma("sync", "ld%d" % par, xt_b[par], V["xT"][:, :, t * TL:(t + 1) * TL])
        p.dma("sync", "ld%d" % par, s0_b[par], V["s0"][:, :, t * TL:(t + 1) * TL])
        ld_mark[t] = p.dma("sync", "ld%d" % par, cp_b[par],
                           V["cp"][:, :, t * TL:(t + 1) * TL])

    b1_load(0)
    for t in range(NT):
        par = t % 2
        ldsem = "ld%d" % par
        p.wait("tensor", ldsem, ld_mark[t])
        for e in range(KC):
            slot = e % 4
            bk = PBIG[:, slot * 512:slot * 512 + TL]
            if slot in free_bank:
                p.wait("tensor", "scl", free_bank[slot])
            for k in range(KC):
                last = k == KC - 1
                p.op("tensor", lambda E, k=k, e=e, bk=bk, x=xt_b[par]: E.matmul(
                    bk, wg[:, k, e * 128:(e + 1) * 128], x[:, k, :],
                    start=(k == 0), stop=last), sem="mm" if last else None)
            p.wait("scalar", "mm", p.cnt["mm"])
            if free_g:
                p.wait("scalar", "vec", free_g)
                if debug:
                    p.wait("scalar", "sp")
            p.op("scalar", lambda E, bk=bk, e=e: E.activation(
                gt_b[:, e, :], bk, AF.Sigmoid, bias=C["bgate"][:, e:e + 1],
                scale=np.float32(1.0)), sem="scl")
            free_bank[slot] = p.cnt["scl"]
        if t + 1 < NT:
            b1_load(t + 1)
        p.wait("vector", "scl", p.cnt["scl"])
        p.wait("vector", ldsem, ld_mark[t])
        p.wait("vector", "sp")
        for e in range(KC):
            p.op("vector", lambda E, e=e, par=par: E.scalar_tensor_tensor(
                tmp16[:, e, :], cp_b[par][:, e, :], C["init"][:, e:e + 1],
                s0_b[par][:, e, :], ALU.mult, ALU.add), sem="vec")
        p.wait("vector", "vec", p.cnt["vec"])
        p.op("vector", lambda E: E.tensor_tensor(
            yt_b[:], tmp16[:], gt_b[:], ALU.mult), sem="vec")
        free_g = p.cnt["vec"]
        free_ld[par] = p.cnt["vec"]
        p.wait("sync", "vec", p.cnt["vec"])
        p.dma("sync", "sp", V["y"][:, :, t * TL:(t + 1) * TL], yt_b)
        if debug and t == 0:
            p.dma("sync", "sp", g_dbg.rearrange("(c p) t -> p c t", p=128), gt_b)

    wo = ARB(0, KC, D)
    o = KC * D
    yt2 = [ARB(o, KC, TL), ARB(o + KC * TL, KC, TL)]
    o += 2 * KC * TL
    xt2 = [ARB(o, KC, TL), ARB(o + KC * TL, KC, TL)]
    o += 2 * KC * TL
    rb2 = ARB(o, KC, TL)
    r2b2 = ARB(o + KC * TL, KC, TL)
    rr = ARF(0, KC, TL)
    rows = ARFR(KC * TL, 4 * TL)
    mrow = ARFR(KC * TL + 4 * TL, TL)
    bcb = ARF(KC * TL + 5 * TL, 1, 2 * TL)

    p.wait("gpsimd", "mm", p.cnt["mm"])
    p.dma("gpsimd", "gw", wo, W_out.rearrange("(k p) e -> p k e", p=128))
    p.wait("tensor", "gw")
    p.wait("sync", "sp")

    def ln_rows(rowbuf, mbuf, src_sum, src_sq):
        p.op("vector", lambda E: E.tensor_scalar(
            mbuf, src_sum, 1.0 / D, None, ALU.mult), sem="vec")
        p.wait("vector", "vec", p.cnt["vec"])
        p.op("vector", lambda E: E.tensor_tensor(
            rowbuf[:, 0:TL], mbuf, mbuf, ALU.mult), sem="vec")
        p.wait("vector", "vec", p.cnt["vec"])
        p.op("vector", lambda E: E.scalar_tensor_tensor(
            rowbuf[:, TL:2 * TL], src_sq, 1.0 / D, rowbuf[:, 0:TL],
            ALU.mult, ALU.subtract), sem="vec")
        p.wait("scalar", "vec", p.cnt["vec"])
        p.op("scalar", lambda E: E.activation(
            rowbuf[:, 2 * TL:3 * TL], rowbuf[:, TL:2 * TL], AF.Sqrt,
            bias=np.float32(1e-5), scale=np.float32(1.0)), sem="scl")
        p.wait("vector", "scl", p.cnt["scl"])
        p.op("vector", lambda E: E.reciprocal(
            rowbuf[:, 3 * TL:4 * TL], rowbuf[:, 2 * TL:3 * TL]), sem="vec")
        p.wait("vector", "vec", p.cnt["vec"])
        p.op("vector", lambda E: E.tensor_tensor(
            rowbuf[:, 0:TL], mbuf, rowbuf[:, 3 * TL:4 * TL], ALU.mult), sem="vec")

    free_reg = [0, 0]
    free_ld = [0, 0]
    free_rr = 0
    ld_mark2 = [0] * (NT + 1)

    def b2_load(t):
        par = t % 2
        if free_ld[par]:
            p.wait("sync", "mm", free_ld[par])
        p.dma("sync", "ld%d" % par, yt2[par], V["y"][:, :, t * TL:(t + 1) * TL])
        ld_mark2[t] = p.dma("sync", "ld%d" % par, xt2[par],
                            V["xT"][:, :, t * TL:(t + 1) * TL])

    b2_load(0)
    for t in range(NT):
        par = t % 2
        ldsem = "ld%d" % par
        p.wait("tensor", ldsem, ld_mark2[t])
        if t > 0:
            p.wait("scalar", "sp")
            p.wait("scalar", "gxh")
        for g in range(4):
            reg = g % 2
            if free_reg[reg]:
                p.wait("tensor", "scl", free_reg[reg])
            for ei in range(4):
                e = g * 4 + ei
                bk = PBIG[:, reg * 1024 + ei * TL: reg * 1024 + (ei + 1) * TL]
                for k in range(KC):
                    p.op("tensor", lambda E, k=k, e=e, bk=bk, y=yt2[par]: E.matmul(
                        bk, wo[:, k, e * 128:(e + 1) * 128], y[:, k, :],
                        start=(k == 0), stop=False))
                p.op("tensor", lambda E, e=e, bk=bk, x=xt2[par]: E.matmul(
                    bk, C["idb"], x[:, e, :], start=False, stop=True),
                    sem="mm" if ei == 3 else None)
            p.wait("scalar", "mm", p.cnt["mm"])
            if free_rr:
                p.wait("scalar", "vec", free_rr)
            for ei in range(4):
                e = g * 4 + ei
                p.op("scalar", lambda E, g=g, reg=reg, ei=ei, e=e: E.activation(
                    rr[:, e, :],
                    PBIG[:, reg * 1024 + ei * TL: reg * 1024 + (ei + 1) * TL],
                    AF.Identity, bias=C["bout"][:, e:e + 1],
                    scale=np.float32(1.0)), sem="scl")
            free_reg[reg] = p.cnt["scl"]
        free_ld[par] = p.cnt["mm"]
        if t + 1 < NT:
            b2_load(t + 1)
        p.wait("scalar", "scl", p.cnt["scl"])
        p.op("scalar", lambda E: E.copy(rb2[:], rr[:]), sem="scl")
        p.op("scalar", lambda E: E.square(r2b2[:], rr[:]), sem="scl")
        p.wait("tensor", "scl", p.cnt["scl"])
        for k in range(KC):
            p.op("tensor", lambda E, k=k: E.matmul(
                PS_ROW[:, 0:TL], C["onesc"], rb2[:, k, :],
                start=(k == 0), stop=(k == KC - 1)))
        for k in range(KC):
            last = k == KC - 1
            p.op("tensor", lambda E, k=k: E.matmul(
                PS_ROW[:, 512:512 + TL], C["onesc"], r2b2[:, k, :],
                start=(k == 0), stop=last), sem="mm" if last else None)
        p.wait("vector", "mm", p.cnt["mm"])
        ln_rows(rows, mrow, PS_ROW[:, 0:TL], PS_ROW[:, 512:512 + TL])
        p.wait("tensor", "vec", p.cnt["vec"])
        p.op("tensor", lambda E: E.matmul(
            PS2[:, 0:TL], C["ones1"], rows[:, 3 * TL:4 * TL], start=True, stop=True))
        p.op("tensor", lambda E: E.matmul(
            PS2[:, 512:512 + TL], C["ones1"], rows[:, 0:TL], start=True, stop=True),
            sem="mm")
        p.wait("scalar", "mm", p.cnt["mm"])
        p.op("scalar", lambda E: E.copy(bcb[:, 0:TL], PS2[:, 0:TL]), sem="scl")
        p.op("scalar", lambda E: E.copy(bcb[:, TL:2 * TL], PS2[:, 512:512 + TL]), sem="scl")
        p.wait("vector", "scl", p.cnt["scl"])
        if t > 0:
            p.wait("vector", "sp")
            p.wait("vector", "gxh")
        p.op("vector", lambda E: E.tensor_tensor(
            rr[:], rr[:], bc3(bcb[:, 0:TL], KC), ALU.mult), sem="vec")
        p.op("vector", lambda E: E.tensor_tensor(
            rr[:], rr[:], bc3(bcb[:, TL:2 * TL], KC), ALU.subtract), sem="vec")
        free_rr = p.cnt["vec"]
        p.wait("sync", "vec", p.cnt["vec"])
        p.dma("sync", "sp", V["xhat"][:, :, t * TL:(t + 1) * TL], rr)
        p.wait("gpsimd", "vec", p.cnt["vec"])
        p.dma("gpsimd", "gxh", V["xhatb"][:, :, t * TL:(t + 1) * TL], rr)

    def ternary_prep(W, wq, nk, nm, col):
        stage_f = [ARF(i * nm, 1, nm) for i in range(2)]
        stage_b = ARB(65536, 1, nm)
        srow = arf_t[0:1, 2 * nm + 1:2 * nm + 3]
        wv = W.rearrange("(k p) m -> p k m", p=128)
        accrow = arf_t[0:1, 2 * nm + 16:2 * nm + 16 + 512]
        last_use = [0, 0]
        nsl = nm // 512
        ldm = [0] * (nk + 1)

        def prep_load(k):
            pr = k % 2
            if last_use[pr]:
                p.wait("sync", "mm", last_use[pr])
            ldm[k] = p.dma("sync", "ld%d" % pr, stage_f[pr], wv[:, k, :])

        prep_load(0)
        for k in range(nk):
            pr = k % 2
            p.wait("scalar", "ld%d" % pr, ldm[k])
            p.wait("scalar", "mm", p.cnt["mm"])
            p.op("scalar", lambda E, k=k, pr=pr: E.activation(
                stage_b, stage_f[pr], AF.Abs, bias=np.float32(0.0),
                scale=np.float32(1.0)), sem="scl")
            if k + 1 < nk:
                prep_load(k + 1)
            p.wait("tensor", "scl", p.cnt["scl"])
            if k >= 1:
                p.wait("tensor", "vec", p.cnt["vec"])
            for s_ in range(nsl):
                p.op("tensor", lambda E, s_=s_: E.matmul(
                    PS_ROW[0:1, 0:512], C["onesc"],
                    stage_b[:, s_ * 512:(s_ + 1) * 512],
                    start=(s_ == 0), stop=(s_ == nsl - 1)),
                    sem="mm" if s_ == nsl - 1 else None)
            last_use[pr] = p.cnt["mm"]
            p.wait("vector", "mm", p.cnt["mm"])
            if k == 0:
                p.op("vector", lambda E: E.tensor_copy(
                    accrow, PS_ROW[0:1, 0:512]), sem="vec")
            else:
                p.op("vector", lambda E: E.tensor_tensor(
                    accrow, accrow, PS_ROW[0:1, 0:512], ALU.add), sem="vec")
            p.wait("vector", "vec", p.cnt["vec"])
        p.op("vector", lambda E: E.tensor_reduce(
            srow[:, 0:1], accrow, AX.X, ALU.add), sem="vec")
        p.wait("vector", "vec", p.cnt["vec"])
        p.op("vector", lambda E: E.tensor_scalar(
            srow[:, 0:1], srow[:, 0:1], 1.0 / (nk * 128 * nm), None,
            ALU.mult), sem="vec")
        p.wait("vector", "scl", p.cnt["scl"])
        p.op("vector", lambda E: E.tensor_scalar(
            srow[:, 0:1], srow[:, 0:1], 1e-5, None, ALU.max), sem="vec")
        p.wait("tensor", "vec", p.cnt["vec"])
        p.op("tensor", lambda E: E.matmul(
            PS2[:, 0:1], C["ones1"], srow[:, 0:1], start=True, stop=True), sem="mm")
        p.wait("scalar", "mm", p.cnt["mm"])
        p.op("scalar", lambda E: E.copy(C["alpha"][:, col:col + 1], PS2[:, 0:1]), sem="scl")
        p.wait("vector", "scl", p.cnt["scl"])
        p.op("vector", lambda E: E.reciprocal(
            C["alpha"][:, col + 1:col + 2], C["alpha"][:, col:col + 1]), sem="vec")
        last_q = [0, 0]
        ldm2 = [0] * (nk + 1)

        def prep_load2(k):
            pr = k % 2
            if last_q[pr]:
                p.wait("sync", "vec", last_q[pr])
            ldm2[k] = p.dma("sync", "ld%d" % pr, stage_f[pr], wv[:, k, :])

        prep_load2(0)
        for k in range(nk):
            pr = k % 2
            p.wait("vector", "ld%d" % pr, ldm2[k])
            if k + 1 < nk:
                prep_load2(k + 1)
            p.op("vector", lambda E, k=k, pr=pr: E.tensor_scalar(
                stage_f[pr], stage_f[pr], C["alpha"][:, col + 1:col + 2], MAGIC,
                ALU.mult, ALU.add), sem="vec")
            p.op("vector", lambda E, k=k, pr=pr: E.tensor_scalar(
                stage_f[pr], stage_f[pr], MAGIC, -1.0, ALU.subtract, ALU.max), sem="vec")
            last_q[pr] = p.op("vector", lambda E, k=k, pr=pr: E.tensor_scalar(
                wq[:, k, :], stage_f[pr], 1.0, None, ALU.min), sem="vec")

    wq_up = ARB(0, KC, I)
    R0 = ARB(65536, KC, TL)
    R1 = ARB(65536 + 4096, KC, TL)
    R2 = ARB(65536 + 8192, KC, TL)
    R3b = ARB(65536 + 12288, IC, TL)
    RH = ARB(65536, IC, TL)
    vq = ARF(0, 8, TL)
    hrc4 = ARF(2048, 4, TL)
    tokA = ARF(3072, 1, 2)
    tokB = ARF(3074, 1, 2)
    tokC_ = ARF(3076, 1, 2)
    tokD_ = ARF(3078, 1, 2)
    tokE = ARF(3080, 1, 2)
    tokF = ARF(3082, 1, 2)
    tokG = ARF(3084, 1, 2)
    tokH = ARF(3086, 1, 2)
    srowc = ARFR(3200, 256)
    csbc = ARF(3328, 1, TL)
    cs2bc = ARF(3584, 1, TL)

    p.wait("sync", "gxh")
    p.wait("sync", "sp")
    ternary_prep(w_upT, wq_up, KC, I, 0)

    free_R0 = 0
    free_RH_v = 0
    upfree = [0, 0]
    h128free = [0, 0]
    ldC_mark = [0] * (NT + 1)

    def c_load(t):
        p.wait("sync", "vec", p.cnt["vec"])
        ldC_mark[t] = p.dma("sync", "ld0", R0,
                            V["xhatb"][:, :, t * TL:(t + 1) * TL])

    for t in range(NT):
        c_load(t)
        p.wait("vector", "ld0", ldC_mark[t])
        p.wait("vector", "gxh")
        if free_RH_v:
            p.wait("vector", "sp")
        bfly(p, R0, R1, KC, 8)
        bfly(p, R1, R2, KC, 4)
        bfly(p, R2, R1, KC, 2)
        bfly(p, R1, R2, KC, 1)
        p.wait("tensor", "vec", p.cnt["vec"])
        for g in range(4):
            reg = g % 2
            if h128free[reg]:
                p.wait("tensor", "scl", h128free[reg])
            for c4 in range(4):
                cc = g * 4 + c4
                p.op("tensor", lambda E, cc=cc, c4=c4, reg=reg: E.matmul(
                    PBIG[:, reg * 1024 + c4 * TL:reg * 1024 + (c4 + 1) * TL],
                    C["H"], R2[:, cc, :], start=True, stop=True),
                    sem="mm" if c4 == 3 else None)
            p.wait("scalar", "mm", p.cnt["mm"])
            p.op("scalar", lambda E, g=g, reg=reg: E.copy(
                R0[:, 4 * g:4 * g + 4, :].rearrange("p c w -> p (c w)"),
                PBIG[:, reg * 1024:reg * 1024 + 1024]), sem="scl")
            p.op("scalar", lambda E, g=g, reg=reg: E.activation(
                R1[:, 4 * g:4 * g + 4, :].rearrange("p c w -> p (c w)"),
                PBIG[:, reg * 1024:reg * 1024 + 1024], AF.Abs,
                bias=np.float32(0.0), scale=np.float32(1.0)), sem="scl")
            h128free[reg] = p.cnt["scl"]
        p.wait("vector", "scl", p.cnt["scl"])
        p.op("vector", lambda E: E.tensor_tensor(
            vq[:], R1[:, 0:8, :], R1[:, 8:16, :], ALU.max), sem="vec")
        p.wait("vector", "vec", p.cnt["vec"])
        p.op("vector", lambda E: E.tensor_tensor(
            vq[:, 0:4, :], vq[:, 0:4, :], vq[:, 4:8, :], ALU.max), sem="vec")
        p.wait("vector", "vec", p.cnt["vec"])
        p.op("vector", lambda E: E.tensor_tensor(
            vq[:, 0:2, :], vq[:, 0:2, :], vq[:, 2:4, :], ALU.max), sem="vec")
        p.wait("vector", "vec", p.cnt["vec"])
        p.op("vector", lambda E: E.tensor_tensor(
            vq[:, 0:1, :], vq[:, 0:1, :], vq[:, 1:2, :], ALU.max), sem="vec")
        p.wait("tensor", "vec", p.cnt["vec"])
        p.op("tensor", lambda E: E.transpose(PS2[:, 0:128], vq[:, 0, 0:128], C["idf"]))
        p.op("tensor", lambda E: E.transpose(PS2[:, 128:256], vq[:, 0, 128:256], C["idf"]),
             sem="mm")
        p.wait("vector", "mm", p.cnt["mm"])
        p.op("vector", lambda E: E.tensor_reduce(
            tokA[:, 0:1], PS2[:, 0:128], AX.X, ALU.max), sem="vec")
        p.op("vector", lambda E: E.tensor_reduce(
            tokA[:, 1:2], PS2[:, 128:256], AX.X, ALU.max), sem="vec")
        p.wait("vector", "vec", p.cnt["vec"])
        p.op("vector", lambda E: E.tensor_scalar(
            tokB, tokA, C_D, 1e-5, ALU.mult, ALU.max), sem="vec")
        p.wait("vector", "vec", p.cnt["vec"])
        p.op("vector", lambda E: E.tensor_scalar(
            tokC_, tokB, 1.0 / 127.0, None, ALU.mult), sem="vec")
        p.wait("vector", "vec", p.cnt["vec"])
        p.op("vector", lambda E: E.reciprocal(tokD_, tokC_), sem="vec")
        p.wait("vector", "vec", p.cnt["vec"])
        p.op("vector", lambda E: E.tensor_scalar(
            tokE, tokD_, C_D, None, ALU.mult), sem="vec")
        p.wait("vector", "vec", p.cnt["vec"])
        p.op("vector", lambda E: E.tensor_scalar(
            tokF, tokC_, C["alpha"][:, 0:1], None, ALU.mult), sem="vec")
        p.wait("vector", "vec", p.cnt["vec"])
        p.op("vector", lambda E: E.tensor_tensor(
            tokG, tokF, tokF, ALU.mult), sem="vec")
        p.wait("vector", "vec", p.cnt["vec"])

        p.wait("tensor", "vec", p.cnt["vec"])
        p.op("tensor", lambda E: E.transpose(PS2[0:1, 256:384], tokE[:, 0:1], C["idf"]))
        p.op("tensor", lambda E: E.transpose(PS2[0:1, 384:512], tokE[:, 1:2], C["idf"]), sem="mm")
        p.wait("scalar", "mm", p.cnt["mm"])
        p.op("scalar", lambda E: E.copy(srowc[0:1, :], PS2[0:1, 256:512]), sem="scl")
        p.wait("tensor", "scl", p.cnt["scl"])
        p.op("tensor", lambda E: E.matmul(
            PS2[:, 512:640], C["ones1"], srowc[0:1, 0:128], start=True, stop=True))
        p.op("tensor", lambda E: E.matmul(
            PS2[:, 640:768], C["ones1"], srowc[0:1, 128:256], start=True, stop=True),
            sem="mm")
        p.wait("scalar", "mm", p.cnt["mm"])
        p.op("scalar", lambda E: E.copy(csbc, PS2[:, 512:768]), sem="scl")
        p.wait("vector", "scl", p.cnt["scl"])
        for half in range(2):
            sl = slice(half * 8, half * 8 + 8)
            p.op("vector", lambda E, sl=sl: E.tensor_tensor(
                vq[:], R0[:, sl, :], bc3(csbc, 8), ALU.mult), sem="vec")
            p.op("vector", lambda E: E.tensor_scalar(
                vq[:], vq[:], MAGIC, MAGIC, ALU.add, ALU.subtract), sem="vec")
            p.op("vector", lambda E, sl=sl: E.tensor_scalar(
                R2[:, sl, :], vq[:], -127.0, 127.0, ALU.max, ALU.min), sem="vec")
        free_R0 = p.cnt["vec"]
        p.wait("tensor", "vec", p.cnt["vec"])
        for g in range(8):
            reg = g % 2
            if upfree[reg]:
                p.wait("tensor", "scl", upfree[reg])
            for ii in range(4):
                i_c = g * 4 + ii
                for k in range(KC):
                    last = (k == KC - 1) and (ii == 3)
                    p.op("tensor", lambda E, k=k, i_c=i_c, ii=ii, reg=reg: E.matmul(
                        PBIG[:, reg * 1024 + ii * TL: reg * 1024 + (ii + 1) * TL],
                        wq_up[:, k, i_c * 128:(i_c + 1) * 128], R2[:, k, :],
                        start=(k == 0), stop=(k == KC - 1)),
                        sem="mm" if last else None)
            p.wait("scalar", "mm", p.cnt["mm"])
            p.op("scalar", lambda E, reg=reg: E.activation(
                hrc4[:], PBIG[:, reg * 1024:reg * 1024 + 1024]
                .rearrange("p (c w) -> p c w", c=4), AF.Relu,
                bias=np.float32(0.0), scale=np.float32(1.0)), sem="scl")
            p.op("scalar", lambda E, g=g: E.square(
                R3b[:, 4 * g:4 * g + 4, :].rearrange("p c w -> p (c w)"),
                hrc4[:].rearrange("p c w -> p (c w)")), sem="scl")
            upfree[reg] = p.cnt["scl"]
        p.wait("vector", "scl", p.cnt["scl"])
        bfly(p, R3b, RH, IC, 16)
        bfly(p, RH, R3b, IC, 8)
        bfly(p, R3b, RH, IC, 4)
        bfly(p, RH, R3b, IC, 2)
        bfly(p, R3b, RH, IC, 1)
        p.wait("tensor", "vec", p.cnt["vec"])
        for g in range(8):
            reg = g % 2
            if h128free[reg]:
                p.wait("tensor", "scl", h128free[reg])
            for c4 in range(4):
                cc = g * 4 + c4
                p.op("tensor", lambda E, cc=cc, c4=c4, reg=reg: E.matmul(
                    PBIG[:, reg * 1024 + c4 * TL:reg * 1024 + (c4 + 1) * TL],
                    C["H"], RH[:, cc, :], start=True, stop=True),
                    sem="mm" if c4 == 3 else None)
            p.wait("scalar", "mm", p.cnt["mm"])
            p.op("scalar", lambda E, g=g, reg=reg: E.copy(
                R3b[:, 4 * g:4 * g + 4, :].rearrange("p c w -> p (c w)"),
                PBIG[:, reg * 1024:reg * 1024 + 1024]), sem="scl")
            p.op("scalar", lambda E, g=g, reg=reg: E.activation(
                RH[:, 4 * g:4 * g + 4, :].rearrange("p c w -> p (c w)"),
                PBIG[:, reg * 1024:reg * 1024 + 1024], AF.Abs,
                bias=np.float32(0.0), scale=np.float32(1.0)), sem="scl")
            h128free[reg] = p.cnt["scl"]
        p.wait("vector", "scl", p.cnt["scl"])
        p.op("vector", lambda E: E.tensor_tensor(
            vq[:], RH[:, 0:8, :], RH[:, 8:16, :], ALU.max), sem="vec")
        p.wait("vector", "vec", p.cnt["vec"])
        p.op("vector", lambda E: E.tensor_tensor(
            vq[:], vq[:], RH[:, 16:24, :], ALU.max), sem="vec")
        p.wait("vector", "vec", p.cnt["vec"])
        p.op("vector", lambda E: E.tensor_tensor(
            vq[:], vq[:], RH[:, 24:32, :], ALU.max), sem="vec")
        p.wait("vector", "vec", p.cnt["vec"])
        p.op("vector", lambda E: E.tensor_tensor(
            vq[:, 0:4, :], vq[:, 0:4, :], vq[:, 4:8, :], ALU.max), sem="vec")
        p.wait("vector", "vec", p.cnt["vec"])
        p.op("vector", lambda E: E.tensor_tensor(
            vq[:, 0:2, :], vq[:, 0:2, :], vq[:, 2:4, :], ALU.max), sem="vec")
        p.wait("vector", "vec", p.cnt["vec"])
        p.op("vector", lambda E: E.tensor_tensor(
            vq[:, 0:1, :], vq[:, 0:1, :], vq[:, 1:2, :], ALU.max), sem="vec")
        p.wait("tensor", "vec", p.cnt["vec"])
        p.op("tensor", lambda E: E.transpose(PS2[:, 0:128], vq[:, 0, 0:128], C["idf"]))
        p.op("tensor", lambda E: E.transpose(PS2[:, 128:256], vq[:, 0, 128:256], C["idf"]),
             sem="mm")
        p.wait("vector", "mm", p.cnt["mm"])
        p.op("vector", lambda E: E.tensor_reduce(
            tokA[:, 0:1], PS2[:, 0:128], AX.X, ALU.max), sem="vec")
        p.op("vector", lambda E: E.tensor_reduce(
            tokA[:, 1:2], PS2[:, 128:256], AX.X, ALU.max), sem="vec")
        p.wait("vector", "vec", p.cnt["vec"])
        p.op("vector", lambda E: E.tensor_scalar(
            tokB, tokA, C_I, 1e-5, ALU.mult, ALU.max), sem="vec")
        p.wait("vector", "vec", p.cnt["vec"])
        p.op("vector", lambda E: E.tensor_scalar(
            tokC_, tokB, 1.0 / 127.0, None, ALU.mult), sem="vec")
        p.wait("vector", "vec", p.cnt["vec"])
        p.op("vector", lambda E: E.reciprocal(tokD_, tokC_), sem="vec")
        p.wait("vector", "vec", p.cnt["vec"])
        p.op("vector", lambda E: E.tensor_scalar(
            tokE, tokD_, C_I, None, ALU.mult), sem="vec")
        p.wait("vector", "vec", p.cnt["vec"])
        p.op("vector", lambda E: E.tensor_tensor(
            tokH, tokG, tokC_, ALU.mult), sem="vec")
        p.wait("vector", "vec", p.cnt["vec"])

        p.wait("sync", "vec", p.cnt["vec"])
        p.dma("sync", "sp", fold_sp[t], tokH)
        p.wait("tensor", "vec", p.cnt["vec"])
        p.op("tensor", lambda E: E.transpose(PS2[0:1, 256:384], tokE[:, 0:1], C["idf"]))
        p.op("tensor", lambda E: E.transpose(PS2[0:1, 384:512], tokE[:, 1:2], C["idf"]), sem="mm")
        p.wait("scalar", "mm", p.cnt["mm"])
        p.op("scalar", lambda E: E.copy(srowc[0:1, :], PS2[0:1, 256:512]), sem="scl")
        p.wait("tensor", "scl", p.cnt["scl"])
        p.op("tensor", lambda E: E.matmul(
            PS2[:, 512:640], C["ones1"], srowc[0:1, 0:128], start=True, stop=True))
        p.op("tensor", lambda E: E.matmul(
            PS2[:, 640:768], C["ones1"], srowc[0:1, 128:256], start=True, stop=True),
            sem="mm")
        p.wait("scalar", "mm", p.cnt["mm"])
        p.op("scalar", lambda E: E.copy(cs2bc, PS2[:, 512:768]), sem="scl")
        p.wait("vector", "scl", p.cnt["scl"])
        for q4 in range(4):
            sl = slice(q4 * 8, q4 * 8 + 8)
            p.op("vector", lambda E, sl=sl: E.tensor_tensor(
                vq[:], R3b[:, sl, :], bc3(cs2bc, 8), ALU.mult), sem="vec")
            p.op("vector", lambda E: E.tensor_scalar(
                vq[:], vq[:], MAGIC, MAGIC, ALU.add, ALU.subtract), sem="vec")
            p.op("vector", lambda E, sl=sl: E.tensor_scalar(
                RH[:, sl, :], vq[:], -127.0, 127.0, ALU.max, ALU.min), sem="vec")
        free_RH_v = p.cnt["vec"]
        p.wait("sync", "vec", p.cnt["vec"])
        p.dma("sync", "sp", V["xq2"][:, :, t * TL:(t + 1) * TL], RH)

    wq_dn = ARB(0, IC, D)
    xq2t2 = [ARB(65536, IC, TL), ARB(65536 + 8192, IC, TL)]
    rsD = ARB(65536 + 16384, KC, TL)
    xht = ARF(0, KC, TL)
    dnc4 = ARF(4096, 4, TL)
    rowsD = ARFR(5120, 4 * TL)
    mrowD = ARFR(6144, TL)
    foldt = ARF(6144, 1, 2)
    sct = ARF(6146, 1, 2)
    bcD = ARF(6400, 1, 2 * TL)
    srowD = ARFR(6912, 256)
    outT = ARF(7168, 1, D)

    p.wait("gpsimd", "mm", p.cnt["mm"])
    p.wait("sync", "sp")
    ternary_prep(w_dnT, wq_dn, IC, D, 2)
    if debug:
        p.wait("sync", "vec", p.cnt["vec"])
        p.dma("sync", "sp", wqd_dbg, wq_dn)
        p.dma("sync", "sp", alpha_dbg, C["alpha"])

    free_reg = [0, 0]
    free_dnc = 0
    free_xht_mm = 0
    free_xq2 = [0, 0]
    ldD_mark = [0] * (NT + 1)
    ldD2_mark = [0] * (NT + 1)

    def d_load(t):
        par = t % 2
        if free_xq2[par]:
            p.wait("sync", "mm", free_xq2[par])
        ldD_mark[t] = p.dma("sync", "ld%d" % par, xq2t2[par],
                            V["xq2"][:, :, t * TL:(t + 1) * TL])

    d_load(0)
    for t in range(NT):
        par = t % 2
        xq2t = xq2t2[par]
        if free_xht_mm:
            p.wait("sync", "mm", free_xht_mm)
        p.dma("sync", "sp", xht, V["xhat"][:, :, t * TL:(t + 1) * TL])
        ldD2_mark[t] = p.dma("sync", "sp", foldt, fold_sp[t])
        p.wait("vector", "sp", ldD2_mark[t])
        p.op("vector", lambda E: E.tensor_scalar(
            sct, foldt, C["alpha"][:, 2:3], None, ALU.mult), sem="vec")
        sct_vec = p.cnt["vec"]
        p.wait("tensor", "ld%d" % par, ldD_mark[t])
        for g in range(4):
            reg = g % 2
            if free_reg[reg]:
                p.wait("tensor", "scl", free_reg[reg])
            for oi in range(4):
                oc = g * 4 + oi
                for k in range(IC):
                    last = (k == IC - 1) and (oi == 3)
                    p.op("tensor", lambda E, k=k, oc=oc, oi=oi, reg=reg, x=xq2t: E.matmul(
                        PBIG[:, reg * 1024 + oi * TL: reg * 1024 + (oi + 1) * TL],
                        wq_dn[:, k, oc * 128:(oc + 1) * 128], x[:, k, :],
                        start=(k == 0), stop=(k == IC - 1)),
                        sem="mm" if last else None)
            if g == 0:
                p.wait("tensor", "vec", sct_vec)
                p.wait("tensor", "scl", p.cnt["scl"])
                p.op("tensor", lambda E: E.transpose(PS2[0:1, 256:384], sct[:, 0:1], C["idf"]))
                p.op("tensor", lambda E: E.transpose(PS2[0:1, 384:512], sct[:, 1:2], C["idf"]), sem="mm")
                p.wait("scalar", "mm", p.cnt["mm"])
                p.op("scalar", lambda E: E.copy(srowD[0:1, :], PS2[0:1, 256:512]), sem="scl")
                p.wait("tensor", "scl", p.cnt["scl"])
                p.op("tensor", lambda E: E.matmul(
                    PS2[:, 512:640], C["ones1"], srowD[0:1, 0:128], start=True, stop=True))
                p.op("tensor", lambda E: E.matmul(
                    PS2[:, 640:768], C["ones1"], srowD[0:1, 128:256], start=True, stop=True),
                    sem="mm")
                p.wait("scalar", "mm", p.cnt["mm"])
                p.op("scalar", lambda E: E.copy(bcD[:, 0:TL], PS2[:, 512:768]), sem="scl")
                bc_scl = p.cnt["scl"]
            p.wait("scalar", "mm", p.cnt["mm"])
            if free_dnc:
                p.wait("scalar", "vec", free_dnc)
            p.op("scalar", lambda E, reg=reg: E.copy(
                dnc4[:].rearrange("p c w -> p (c w)"),
                PBIG[:, reg * 1024:reg * 1024 + 1024]), sem="scl")
            free_reg[reg] = p.cnt["scl"]
            p.wait("vector", "scl", p.cnt["scl"])
            if g == 0:
                p.wait("vector", "scl", bc_scl)
                p.wait("vector", "sp")
            p.op("vector", lambda E: E.tensor_tensor(
                dnc4[:], dnc4[:], bc3(bcD[:, 0:TL], 4), ALU.mult), sem="vec")
            p.op("vector", lambda E, g=g: E.tensor_tensor(
                xht[:, 4 * g:4 * g + 4, :], xht[:, 4 * g:4 * g + 4, :],
                dnc4[:], ALU.add), sem="vec")
            free_dnc = p.cnt["vec"]
        free_xq2[par] = p.cnt["mm"]
        if t + 1 < NT:
            d_load(t + 1)
        p.wait("scalar", "vec", p.cnt["vec"])
        p.op("scalar", lambda E: E.copy(rsD[:], xht[:]), sem="scl")
        p.wait("tensor", "scl", p.cnt["scl"])
        for k in range(KC):
            last = k == KC - 1
            p.op("tensor", lambda E, k=k: E.matmul(
                PS_ROW[:, 0:TL], C["onesc"], rsD[:, k, :],
                start=(k == 0), stop=last), sem="mm" if last else None)
        p.wait("scalar", "mm", p.cnt["mm"])
        p.op("scalar", lambda E: E.square(rsD[:], xht[:]), sem="scl")
        p.wait("tensor", "scl", p.cnt["scl"])
        for k in range(KC):
            last = k == KC - 1
            p.op("tensor", lambda E, k=k: E.matmul(
                PS_ROW[:, 512:512 + TL], C["onesc"], rsD[:, k, :],
                start=(k == 0), stop=last), sem="mm" if last else None)
        p.wait("vector", "mm", p.cnt["mm"])
        ln_rows(rowsD, mrowD, PS_ROW[:, 0:TL], PS_ROW[:, 512:512 + TL])
        p.wait("tensor", "vec", p.cnt["vec"])
        p.op("tensor", lambda E: E.matmul(
            PS2[:, 0:TL], C["ones1"], rowsD[:, 3 * TL:4 * TL], start=True, stop=True))
        p.op("tensor", lambda E: E.matmul(
            PS2[:, 512:512 + TL], C["ones1"], rowsD[:, 0:TL], start=True, stop=True),
            sem="mm")
        p.wait("scalar", "mm", p.cnt["mm"])
        p.op("scalar", lambda E: E.copy(bcD[:, 0:TL], PS2[:, 0:TL]), sem="scl")
        p.op("scalar", lambda E: E.copy(bcD[:, TL:2 * TL], PS2[:, 512:512 + TL]), sem="scl")
        p.wait("vector", "scl", p.cnt["scl"])
        p.op("vector", lambda E: E.tensor_tensor(
            xht[:], xht[:], bc3(bcD[:, 0:TL], KC), ALU.mult), sem="vec")
        p.op("vector", lambda E: E.tensor_tensor(
            xht[:], xht[:], bc3(bcD[:, TL:2 * TL], KC), ALU.subtract), sem="vec")
        p.wait("tensor", "vec", p.cnt["vec"])
        for tc in range(2):
            for half in range(2):
                p.wait("tensor", "scl", p.cnt["scl"])
                for e8 in range(8):
                    e = half * 8 + e8
                    p.op("tensor", lambda E, e=e, e8=e8, tc=tc: E.transpose(
                        PS2[:, e8 * 128:(e8 + 1) * 128],
                        xht[:, e, tc * 128:(tc + 1) * 128], C["idf"]),
                        sem="mm" if e8 == 7 else None)
                p.wait("scalar", "mm", p.cnt["mm"])
                p.wait("scalar", "out")
                p.op("scalar", lambda E, half=half: E.copy(
                    outT[:, half * 1024:half * 1024 + 1024],
                    PS2[:, 0:1024]), sem="scl")
            p.wait("sync", "scl", p.cnt["scl"])
            p.dma("sync", "out",
                  y_out[t * TL + tc * 128:t * TL + (tc + 1) * 128, :], outT)
        free_xht_mm = p.cnt["mm"]

    p.wait("sync", "out")

    with p.es:
        with nc.Block() as block:
            @block.sync
            def _(E):
                for f in p.ops["sync"]:
                    f(E)

            @block.gpsimd
            def _(E):
                for f in p.ops["gpsimd"]:
                    f(E)

            @block.tensor
            def _(E):
                for f in p.ops["tensor"]:
                    f(E)

            @block.vector
            def _(E):
                for f in p.ops["vector"]:
                    f(E)

            @block.scalar
            def _(E):
                for f in p.ops["scalar"]:
                    f(E)
    return nc


def kernel(**inputs):
    x = np.ascontiguousarray(np.asarray(inputs["L0_attn_x"], np.float32))
    B, S, d = x.shape
    assert (B, S, d) == (2, 16384, 2048)

    def prep_bias(v):
        return np.ascontiguousarray(np.asarray(v, np.float32).reshape(KC, 128).T)

    H = np.array([[1.0]])
    while H.shape[0] < 128:
        H = np.block([[H, H], [H, -H]])
    common = dict(
        W_in=np.ascontiguousarray(np.asarray(inputs["L0_attn_W_in"], np.float32)),
        W_dec=np.ascontiguousarray(np.asarray(inputs["L0_attn_W_decay"], np.float32)),
        W_gate=np.ascontiguousarray(np.asarray(inputs["L0_attn_W_gate"], np.float32)),
        W_out=np.ascontiguousarray(np.asarray(inputs["L0_attn_W_out"], np.float32)),
        w_upT=np.ascontiguousarray(np.asarray(inputs["L0_ffn_w_up"], np.float32).T),
        w_dnT=np.ascontiguousarray(np.asarray(inputs["L0_ffn_w_down"], np.float32).T),
        b_in=prep_bias(inputs["L0_attn_b_in"]),
        b_dec=prep_bias(inputs["L0_attn_b_decay"]),
        b_dbias=prep_bias(inputs["L0_attn_decay_bias"]),
        b_gate=prep_bias(inputs["L0_attn_b_gate"]),
        b_out=prep_bias(inputs["L0_attn_b_out"]),
        H128=H.astype(ml_dtypes.bfloat16),
        identb=np.eye(128).astype(ml_dtypes.bfloat16),
        identf=np.eye(128, dtype=np.float32),
    )
    in_maps = []
    for k in range(NCORES):
        b_, q_ = divmod(k, 4)
        sel = np.zeros((128, NCORES), np.float32)
        sel[:, k] = 1.0
        in_maps.append(dict(
            x=np.ascontiguousarray(x[b_, q_ * T:(q_ + 1) * T, :].T),
            sel=sel, **common))

    nc = build_nc()
    res = run_bass_kernel_spmd(nc, in_maps, core_ids=list(range(NCORES)))
    out = np.empty((B, S, d), np.float32)
    for k in range(NCORES):
        b_, q_ = divmod(k, 4)
        out[b_, q_ * T:(q_ + 1) * T, :] = res.results[k]["y"]
    return out
